# version 01
# baseline (speedup 1.0000x reference)
import os
import numpy as np
import ml_dtypes
from contextlib import ExitStack
import concourse.bass as bass
import concourse.mybir as mybir
from concourse.bass_utils import run_bass_kernel_spmd
from concourse.alu_op_type import AluOpType as OP

F32 = mybir.dt.float32
BF16 = mybir.dt.bfloat16
I32 = mybir.dt.int32
AF = mybir.ActivationFunctionType
AX = mybir.AxisListType
ENGINES = ("sync", "tensor", "vector", "scalar", "gpsimd")

NC = 8
S = 16384
D = 4096
T = 2048
DT = 32
HW = T + 512
CSHIFT = 60.0
SQRT512 = float(np.sqrt(512.0))
EPS = 1e-5
SEMS = ["ld", "st", "mm", "dve", "act", "gp"]


class Prog:
    def __init__(self, nc, stack, sem_names=SEMS):
        self.nc = nc
        self.ops = {e: [] for e in ENGINES}
        self.sems = {n: stack.enter_context(nc.semaphore(n)) for n in sem_names}
        self.cnt = {n: 0 for n in sem_names}

    def on(self, eng, fn):
        self.ops[eng].append(fn)

    def wait(self, eng, sem, val=None):
        v = self.cnt[sem] if val is None else val
        s = self.sems[sem]
        self.on(eng, lambda e, s=s, v=v: e.wait_ge(s, v))

    def step(self, eng, builder, inc=None, amt=1):
        if inc is None:
            self.on(eng, builder)
            return None
        s = self.sems[inc]
        self.cnt[inc] += amt
        self.on(eng, lambda e, b=builder, s=s, a=amt: b(e).then_inc(s, a))
        return self.cnt[inc]

    def chain(self, eng, builder, inc):
        v = self.step(eng, builder, inc=inc)
        self.wait(eng, inc, v)
        return v

    def finalize(self):
        with self.nc.Block() as block:
            for e in ENGINES:
                def emit(eng, _e=e):
                    for f in self.ops[_e]:
                        f(eng)
                getattr(block, e)(emit)


def build_L0a():
    nc = bass.Bass("TRN2", target_bir_lowering=False)
    stack = ExitStack()
    s = stack.enter_context
    x = nc.dram_tensor("x", [T, D], F32, kind="ExternalInput").ap()
    wqkvT = nc.dram_tensor("wqkvT", [D, 1536], F32, kind="ExternalInput").ap()
    bias12 = nc.dram_tensor("bias12", [128, 12], F32, kind="ExternalInput").ap()
    qT_o = nc.dram_tensor("qT_o", [512, T], BF16, kind="ExternalOutput").ap()
    kT_o = nc.dram_tensor("kT_o", [512, T], BF16, kind="ExternalOutput").ap()
    v_o = nc.dram_tensor("v_o", [T, 512], BF16, kind="ExternalOutput").ap()
    xnT_o = nc.dram_tensor("xnT_o", [D, T], BF16, kind="ExternalOutput").ap()
    xT_o = nc.dram_tensor("xT_o", [D, T], F32, kind="ExternalOutput").ap()

    xt = s(nc.sbuf_tensor("xt", [128, D], F32))
    hi = s(nc.sbuf_tensor("hi", [128, D], BF16))
    lo = s(nc.sbuf_tensor("lo", [128, D], BF16))
    sq = s(nc.sbuf_tensor("sq", [128, D], BF16))
    xn = s(nc.sbuf_tensor("xn", [128, D], BF16))
    st1 = s(nc.sbuf_tensor("st1", [128, 8], F32))
    xnT = s(nc.sbuf_tensor("xnT", [128, DT * T], BF16))
    ttmp = s(nc.sbuf_tensor("ttmp", [128, 128], F32))
    ttmp2 = s(nc.sbuf_tensor("ttmp2", [128, 128], F32))
    wz = s(nc.sbuf_tensor("wz", [128, 2048], F32))
    wb = [s(nc.sbuf_tensor(f"wb{i}", [128, DT * 128], BF16)) for i in range(2)]
    bias_sb = s(nc.sbuf_tensor("bias_sb", [128, 12], F32))
    ident = s(nc.sbuf_tensor("ident", [128, 128], BF16))
    ev = s(nc.sbuf_tensor("ev", [128, 512], BF16))
    vtk = s(nc.sbuf_tensor("vtk", [128, 128], BF16))
    iota_f = s(nc.sbuf_tensor("iota_f", [128, 128], F32))
    iota_p = s(nc.sbuf_tensor("iota_p", [128, 1], F32))

    pA = s(nc.psum_tensor("pA", [128, 128], BF16))
    pB = s(nc.psum_tensor("pB", [128, 128], BF16))
    pC = s(nc.psum_tensor("pC", [128, 128], BF16))
    pP = s(nc.psum_tensor("pP", [128, 2048], F32))

    p = Prog(nc, stack)
    p.step("gpsimd", lambda e: e.iota(iota_f[:], [[1, 128]], channel_multiplier=0, allow_small_or_imprecise_dtypes=True), inc="gp")
    p.step("gpsimd", lambda e: e.iota(iota_p[:], [[0, 1]], channel_multiplier=1, allow_small_or_imprecise_dtypes=True), inc="gp")
    p.step("sync", lambda e: e.dma_start(out=bias_sb[:], in_=bias12), inc="ld", amt=16)
    p.wait("vector", "gp")
    p.step("vector", lambda e: e.tensor_scalar(ident[:], iota_f[:], iota_p[:], None, OP.is_equal), inc="dve")

    for tt in range(16):
        vld = p.step("sync", lambda e, tt=tt: e.dma_start(out=xt[:], in_=x[tt * 128:(tt + 1) * 128, :]), inc="ld", amt=16)
        p.wait("vector", "ld", vld)
        p.chain("vector", lambda e: e.tensor_copy(hi[:], xt[:]), inc="dve")
        vl = p.chain("vector", lambda e: e.tensor_tensor(lo[:], xt[:], hi[:], OP.subtract), inc="dve")
        p.chain("vector", lambda e: e.reduce_sum(st1[:, 0:1], xt[:], axis=AX.X), inc="dve")
        p.wait("scalar", "ld", vld)
        a1 = p.step("scalar", lambda e: e.activation(sq[:], xt[:], AF.Square), inc="act")
        p.wait("vector", "act", a1)
        p.chain("vector", lambda e: e.reduce_sum(st1[:, 1:2], sq[:], axis=AX.X), inc="dve")
        p.chain("vector", lambda e: e.tensor_scalar(st1[:, 2:3], st1[:, 0:1], 1.0 / D, None, OP.mult), inc="dve")
        p.chain("vector", lambda e: e.tensor_tensor(st1[:, 3:4], st1[:, 2:3], st1[:, 2:3], OP.mult), inc="dve")
        v5 = p.chain("vector", lambda e: e.tensor_scalar(st1[:, 4:5], st1[:, 1:2], 1.0 / D, st1[:, 3:4], OP.mult, OP.subtract), inc="dve")
        p.wait("scalar", "dve", v5)
        a2 = p.step("scalar", lambda e: e.activation(st1[:, 5:6], st1[:, 4:5], AF.Sqrt, bias=np.float32(EPS)), inc="act")
        p.wait("vector", "act", a2)
        p.chain("vector", lambda e: e.reciprocal(st1[:, 6:7], st1[:, 5:6]), inc="dve")
        v7 = p.chain("vector", lambda e: e.tensor_scalar(xn[:], xt[:], st1[:, 2:3], st1[:, 6:7], OP.subtract, OP.mult), inc="dve")
        p.wait("tensor", "dve", v7)
        for dc in range(DT):
            mh = p.step("tensor", lambda e, dc=dc: e.transpose(pA[:], hi[:, dc * 128:(dc + 1) * 128], ident[:]), inc="mm")
            ml = p.step("tensor", lambda e, dc=dc: e.transpose(pB[:], lo[:, dc * 128:(dc + 1) * 128], ident[:]), inc="mm")
            mx = p.step("tensor", lambda e, dc=dc: e.transpose(pC[:], xn[:, dc * 128:(dc + 1) * 128], ident[:]), inc="mm")
            p.wait("scalar", "mm", mh)
            a3 = p.step("scalar", lambda e: e.activation(ttmp[:], pA[:], AF.Copy), inc="act")
            p.wait("vector", "mm", ml)
            p.wait("vector", "act", a3)
            v8 = p.step("vector", lambda e: e.tensor_tensor(ttmp2[:], ttmp[:], pB[:], OP.add), inc="dve")
            p.wait("scalar", "mm", mx)
            a4 = p.step("scalar", lambda e, tt=tt, dc=dc: e.activation(xnT[:, dc * T + tt * 128: dc * T + (tt + 1) * 128], pC[:], AF.Copy), inc="act")
            p.wait("gpsimd", "dve", v8)
            p.step("gpsimd", lambda e, tt=tt, dc=dc: e.dma_start(out=xT_o[dc * 128:(dc + 1) * 128, tt * 128:(tt + 1) * 128], in_=ttmp2[:]), inc="st", amt=16)
            p.wait("tensor", "act", a4)
            p.wait("tensor", "dve", v8)
            p.wait("scalar", "st")
            p.wait("vector", "st")
        p.wait("sync", "dve")
        p.wait("sync", "act")
        p.wait("vector", "mm")
    for dc in range(DT):
        p.step("gpsimd", lambda e, dc=dc: e.dma_start(out=xnT_o[dc * 128:(dc + 1) * 128, :], in_=xnT[:, dc * T:(dc + 1) * T]), inc="st", amt=16)

    wguard = {}
    for ot in range(12):
        par = ot % 2
        for half in range(2):
            for g in range(2):
                p.step("sync", lambda e, ot=ot, half=half, g=g: e.dma_start(
                    out=wz[:, g * 1024:(g + 1) * 1024].rearrange("p (c o) -> p c o", o=128),
                    in_=wqkvT[:, ot * 128:(ot + 1) * 128].rearrange("(c p) o -> p c o", p=128)[:, (half * 2 + g) * 8:(half * 2 + g + 1) * 8, :]), inc="ld", amt=16)
            lv = p.cnt["ld"]
            p.wait("scalar", "ld", lv)
            if half == 0 and par in wguard:
                p.wait("scalar", "mm", wguard[par])
            cv = p.step("scalar", lambda e, par=par, half=half: e.copy(wb[par][:, half * 2048:(half + 1) * 2048], wz[:]), inc="act")
            p.wait("sync", "act", cv)
        p.wait("tensor", "act", cv)
        for tc in range(4):
            for ic in range(DT):
                p.step("tensor", lambda e, par=par, ic=ic, tc=tc: e.matmul(
                    pP[:, tc * 512:(tc + 1) * 512], wb[par][:, ic * 128:(ic + 1) * 128],
                    xnT[:, ic * T + tc * 512: ic * T + tc * 512 + 512],
                    start=(ic == 0), stop=(ic == DT - 1)), inc="mm")
            mdone = p.cnt["mm"]
            p.wait("scalar", "mm", mdone)
            ae = p.step("scalar", lambda e, ot=ot, tc=tc: e.activation(ev[:, 0:512], pP[:, tc * 512:(tc + 1) * 512], AF.Identity, bias=bias_sb[:, ot:ot + 1]), inc="act")
            p.wait("gpsimd", "act", ae)
            if ot < 4:
                p.step("gpsimd", lambda e, ot=ot, tc=tc: e.dma_start(out=qT_o[ot * 128:(ot + 1) * 128, tc * 512:(tc + 1) * 512], in_=ev[:, 0:512]), inc="st", amt=16)
            elif ot < 8:
                p.step("gpsimd", lambda e, ot=ot, tc=tc: e.dma_start(out=kT_o[(ot - 4) * 128:(ot - 3) * 128, tc * 512:(tc + 1) * 512], in_=ev[:, 0:512]), inc="st", amt=16)
            else:
                for j in range(4):
                    mt = p.step("tensor", lambda e, j=j: e.transpose(pC[:], ev[:, j * 128:(j + 1) * 128], ident[:]), inc="mm")
                    p.wait("scalar", "mm", mt)
                    av = p.step("scalar", lambda e: e.activation(vtk[:], pC[:], AF.Copy), inc="act")
                    p.wait("gpsimd", "act", av)
                    p.step("gpsimd", lambda e, ot=ot, tc=tc, j=j: e.dma_start(
                        out=v_o[tc * 512 + j * 128: tc * 512 + (j + 1) * 128, (ot - 8) * 128:(ot - 7) * 128], in_=vtk[:]), inc="st", amt=16)
                    p.wait("tensor", "act", av)
                    p.wait("scalar", "st")
            p.wait("tensor", "act", p.cnt["act"])
            p.wait("scalar", "st")
        wguard[par] = p.cnt["mm"]
    p.wait("gpsimd", "st")
    p.wait("gpsimd", "mm")
    p.wait("gpsimd", "dve")
    p.wait("gpsimd", "act")
    p.finalize()
    return nc, stack


def build_L0b():
    nc = bass.Bass("TRN2", target_bir_lowering=False)
    stack = ExitStack()
    s = stack.enter_context
    qT = nc.dram_tensor("qT", [512, T], BF16, kind="ExternalInput").ap()
    kTf = nc.dram_tensor("kTf", [512, S], BF16, kind="ExternalInput").ap()
    vf = nc.dram_tensor("vf", [S, 512], BF16, kind="ExternalInput").ap()
    xnTd = nc.dram_tensor("xnTd", [D, T], BF16, kind="ExternalInput").ap()
    xTd = nc.dram_tensor("xTd", [D, T], F32, kind="ExternalInput").ap()
    qwr = nc.dram_tensor("qwr", [D, D], I32, kind="ExternalInput").ap()
    scz = nc.dram_tensor("scz", [128, 4 * DT], F32, kind="ExternalInput").ap()
    woT = nc.dram_tensor("woT", [512, D], F32, kind="ExternalInput").ap()
    shifts = nc.dram_tensor("shifts", [128, 1], I32, kind="ExternalInput").ap()
    x1T_o = nc.dram_tensor("x1T_o", [D, T], F32, kind="ExternalOutput").ap()

    xnT = s(nc.sbuf_tensor("xnT", [128, DT * T], BF16))
    acat = s(nc.sbuf_tensor("acat", [128, 4 * T], BF16))
    kh = [s(nc.sbuf_tensor(f"kh{i}", [64, 2048], BF16)) for i in range(2)]
    vaug = s(nc.sbuf_tensor("vaug", [128, 128 * 65], BF16))
    qs = s(nc.sbuf_tensor("qs", [64, 512], BF16))
    pT = [s(nc.sbuf_tensor(f"pT{i}", [128, 512], BF16)) for i in range(2)]
    l0 = s(nc.sbuf_tensor("l0", [1, 512], F32))
    l1 = s(nc.sbuf_tensor("l1", [1, 512], F32))
    oc = s(nc.sbuf_tensor("oc", [64, 512], F32))
    rsb = s(nc.sbuf_tensor("rsb", [128, T], BF16))
    rs_row = s(nc.sbuf_tensor("rs_row", [1, T], F32))
    ones_b = s(nc.sbuf_tensor("ones_b", [128, 1], BF16))
    ones_f = s(nc.sbuf_tensor("ones_f", [1, 128], F32))
    sh_sb = s(nc.sbuf_tensor("sh_sb", [128, 1], I32))
    scz_sb = s(nc.sbuf_tensor("scz_sb", [128, 4 * DT], F32))
    qwt = [s(nc.sbuf_tensor(f"qwt{i}", [128, 128], I32)) for i in range(2)]
    qbi = [s(nc.sbuf_tensor(f"qbi{i}", [128, 128], I32)) for i in range(2)]
    qbf = [s(nc.sbuf_tensor(f"qbf{i}", [128, 128], BF16)) for i in range(2)]
    wof = [s(nc.sbuf_tensor(f"wof{i}", [128, 128], F32)) for i in range(2)]
    wobf = [s(nc.sbuf_tensor(f"wobf{i}", [128, 128], BF16)) for i in range(2)]
    xts = s(nc.sbuf_tensor("xts", [128, 512], F32))
    e1 = s(nc.sbuf_tensor("e1", [128, 512], F32))
    e2 = s(nc.sbuf_tensor("e2", [128, 512], F32))
    e3 = s(nc.sbuf_tensor("e3", [128, 512], F32))
    e4 = s(nc.sbuf_tensor("e4", [128, 512], F32))
    e5 = s(nc.sbuf_tensor("e5", [128, 512], F32))

    pS = [s(nc.psum_tensor(f"pS{i}", [128, 512], F32)) for i in range(2)]
    pO = s(nc.psum_tensor("pO", [128, 512], F32))
    pBc = s(nc.psum_tensor("pBc", [128, 512], F32))
    pF = s(nc.psum_tensor("pF", [128, 1024], F32))
    pW = s(nc.psum_tensor("pW", [128, 1024], F32))

    p = Prog(nc, stack)
    p.step("sync", lambda e: e.dma_start(out=sh_sb[:], in_=shifts), inc="ld", amt=16)
    p.step("sync", lambda e: e.dma_start(out=scz_sb[:], in_=scz), inc="ld", amt=16)
    for dc in range(DT):
        p.step("sync", lambda e, dc=dc: e.dma_start(out=xnT[:, dc * T:(dc + 1) * T], in_=xnTd[dc * 128:(dc + 1) * 128, :]), inc="ld", amt=16)
    p.step("gpsimd", lambda e: e.memset(ones_b[:], 1.0), inc="gp")
    p.step("gpsimd", lambda e: e.memset(ones_f[:], 1.0), inc="gp")
    ldpre = p.cnt["ld"]
    p.wait("tensor", "ld", ldpre)
    p.wait("tensor", "gp")

    for tc in range(4):
        for ic in range(DT):
            p.step("tensor", lambda e, tc=tc, ic=ic: e.matmul(
                pS[0][0:1, :], ones_b[:], xnT[:, ic * T + tc * 512: ic * T + tc * 512 + 512],
                start=(ic == 0), stop=(ic == DT - 1)), inc="mm")
        mdone = p.cnt["mm"]
        p.wait("vector", "mm", mdone)
        v = p.step("vector", lambda e, tc=tc: e.tensor_copy(rs_row[:, tc * 512:(tc + 1) * 512], pS[0][0:1, :]), inc="dve")
        p.wait("tensor", "dve", v)
    for tc in range(4):
        m = p.step("tensor", lambda e, tc=tc: e.matmul(pBc[:], ones_f[:], rs_row[0:1, tc * 512:(tc + 1) * 512], start=True, stop=True), inc="mm")
        p.wait("scalar", "mm", m)
        a = p.step("scalar", lambda e, tc=tc: e.activation(rsb[:, tc * 512:(tc + 1) * 512], pBc[:], AF.Copy), inc="act")
        p.wait("tensor", "act", a)

    kguard = {}
    for h in range(8):
        p.wait("gpsimd", "mm")
        gv = p.step("gpsimd", lambda e: e.memset(vaug[:], 1.0), inc="gp")
        p.wait("sync", "gp", gv)
        for kg in range(8):
            vld = p.step("sync", lambda e, h=h, kg=kg: e.dma_start(
                out=vaug.rearrange("p (c w) -> p c w", w=65)[:, kg * 16:(kg + 1) * 16, 0:64],
                in_=vf.rearrange("(c p) d -> p c d", p=128)[:, kg * 16:(kg + 1) * 16, h * 64:(h + 1) * 64]), inc="ld", amt=16)
        vaug_ready = p.cnt["ld"]
        for qg in range(4):
            qld = p.step("sync", lambda e, h=h, qg=qg: e.dma_start(out=qs[:], in_=qT[h * 64:(h + 1) * 64, qg * 512:(qg + 1) * 512]), inc="ld", amt=16)
            p.wait("tensor", "ld", qld)
            p.wait("tensor", "ld", vaug_ready)
            for piece in range(8):
                par = piece % 2
                if par in kguard:
                    p.wait("sync", "mm", kguard[par])
                kld = p.step("sync", lambda e, h=h, piece=piece, par=par: e.dma_start(
                    out=kh[par][:], in_=kTf[h * 64:(h + 1) * 64, piece * 2048:(piece + 1) * 2048]), inc="ld", amt=16)
                p.wait("tensor", "ld", kld)
                for kc in range(16):
                    ppar = kc % 2
                    mS = p.step("tensor", lambda e, kc=kc, par=par, ppar=ppar: e.matmul(
                        pS[ppar][:], kh[par][:, kc * 128:(kc + 1) * 128], qs[:], start=True, stop=True), inc="mm")
                    p.wait("scalar", "mm", mS)
                    aE = p.step("scalar", lambda e, ppar=ppar: e.activation(pT[ppar][:], pS[ppar][:], AF.Exp, bias=np.float32(-CSHIFT)), inc="act")
                    p.wait("tensor", "act", aE)
                    kabs = piece * 16 + kc
                    p.step("tensor", lambda e, kabs=kabs, ppar=ppar: e.matmul(
                        pO[0:65, :], vaug[:, kabs * 65: kabs * 65 + 65], pT[ppar][:],
                        start=(kabs == 0), stop=(kabs == 127)), inc="mm")
                kguard[par] = p.cnt["mm"]
            av_done = p.cnt["mm"]
            p.wait("vector", "mm", av_done)
            p.chain("vector", lambda e: e.tensor_copy(l0[:], pO[64:65, :]), inc="dve")
            p.chain("vector", lambda e: e.tensor_scalar(l1[:], l0[:], SQRT512, None, OP.mult), inc="dve")
            v3 = p.chain("vector", lambda e: e.reciprocal(l0[:], l1[:]), inc="dve")
            p.wait("tensor", "dve", v3)
            mB = p.step("tensor", lambda e: e.matmul(pBc[0:64, :], ones_f[0:1, 0:64], l0[:], start=True, stop=True), inc="mm")
            p.wait("scalar", "mm", av_done)
            aO = p.step("scalar", lambda e: e.activation(oc[:], pO[0:64, :], AF.Copy), inc="act")
            p.wait("vector", "mm", mB)
            p.wait("vector", "act", aO)
            vz = p.step("vector", lambda e, h=h, qg=qg: e.tensor_tensor(
                acat[(h % 2) * 64:(h % 2) * 64 + 64, (h // 2) * T + qg * 512:(h // 2) * T + qg * 512 + 512],
                oc[:], pBc[0:64, :], OP.mult), inc="dve")
            p.wait("tensor", "act", aO)
            p.wait("tensor", "dve", vz)
            p.wait("scalar", "dve", vz)
            p.wait("sync", "mm", av_done)

    qguard = {}
    woguard = {}
    for ot in range(DT):
        wo_cv = []
        for hc in range(4):
            par = hc % 2
            wld = p.step("sync", lambda e, ot=ot, hc=hc, par=par: e.dma_start(out=wof[par][:], in_=woT[hc * 128:(hc + 1) * 128, ot * 128:(ot + 1) * 128]), inc="ld", amt=16)
            p.wait("scalar", "ld", wld)
            if par in woguard:
                p.wait("scalar", "mm", woguard[par])
            cv = p.step("scalar", lambda e, par=par: e.copy(wobf[par][:], wof[par][:]), inc="act")
            wo_cv.append(cv)
            p.wait("sync", "act", cv)
        for tc in range(4):
            fpar = tc % 2
            for ic in range(DT):
                qpar = ic % 2
                if qpar in qguard:
                    p.wait("vector", "mm", qguard[qpar])
                qld = p.step("sync", lambda e, ot=ot, ic=ic, qpar=qpar: e.dma_start(out=qwt[qpar][:], in_=qwr[ic * 128:(ic + 1) * 128, ot * 128:(ot + 1) * 128]), inc="ld", amt=16)
                p.wait("vector", "ld", qld)
                u1 = p.chain("vector", lambda e, qpar=qpar: e.tensor_scalar(qbi[qpar][:], qwt[qpar][:], sh_sb[:], 15, OP.logical_shift_right, OP.bitwise_and), inc="dve")
                u2 = p.step("vector", lambda e, qpar=qpar: e.tensor_copy(qbf[qpar][:], qbi[qpar][:]), inc="dve")
                p.wait("tensor", "dve", u2)
                p.step("tensor", lambda e, ic=ic, tc=tc, fpar=fpar, qpar=qpar: e.matmul(
                    pF[:, fpar * 512:(fpar + 1) * 512], qbf[qpar][:],
                    xnT[:, ic * T + tc * 512: ic * T + tc * 512 + 512], start=(ic == 0), stop=(ic == DT - 1)), inc="mm")
                qguard[qpar] = p.cnt["mm"]
                p.wait("sync", "dve", u1)
            ffn_done = p.cnt["mm"]
            for hc in range(4):
                p.wait("tensor", "act", wo_cv[hc])
                p.step("tensor", lambda e, hc=hc, tc=tc, fpar=fpar: e.matmul(
                    pW[:, fpar * 512:(fpar + 1) * 512], wobf[hc % 2][:],
                    acat[:, hc * T + tc * 512: hc * T + tc * 512 + 512], start=(hc == 0), stop=(hc == 3)), inc="mm")
            wo_done = p.cnt["mm"]
            p.wait("vector", "mm", ffn_done)
            w1 = p.chain("vector", lambda e, ot=ot, fpar=fpar: e.tensor_scalar(
                e1[:], pF[:, fpar * 512:(fpar + 1) * 512], scz_sb[:, ot:ot + 1], scz_sb[:, 2 * DT + ot:2 * DT + ot + 1], OP.mult, OP.add), inc="dve")
            p.chain("vector", lambda e, ot=ot, tc=tc: e.tensor_scalar(
                e2[:], rsb[:, tc * 512:(tc + 1) * 512], scz_sb[:, DT + ot:DT + ot + 1], None, OP.mult), inc="dve")
            p.chain("vector", lambda e: e.tensor_tensor(e3[:], e1[:], e2[:], OP.subtract), inc="dve")
            p.wait("scalar", "mm", wo_done)
            p.wait("scalar", "dve", p.cnt["dve"])
            w4 = p.step("scalar", lambda e, ot=ot, fpar=fpar: e.activation(
                e4[:], pW[:, fpar * 512:(fpar + 1) * 512], AF.Identity, bias=scz_sb[:, 3 * DT + ot:3 * DT + ot + 1]), inc="act")
            xld = p.step("sync", lambda e, ot=ot, tc=tc: e.dma_start(out=xts[:], in_=xTd[ot * 128:(ot + 1) * 128, tc * 512:(tc + 1) * 512]), inc="ld", amt=16)
            p.wait("vector", "act", w4)
            p.chain("vector", lambda e: e.tensor_tensor(e5[:], e3[:], e4[:], OP.add), inc="dve")
            p.wait("vector", "ld", xld)
            w6 = p.chain("vector", lambda e: e.tensor_tensor(e1[:], e5[:], xts[:], OP.add), inc="dve")
            p.wait("gpsimd", "dve", w6)
            p.step("gpsimd", lambda e, ot=ot, tc=tc: e.dma_start(out=x1T_o[ot * 128:(ot + 1) * 128, tc * 512:(tc + 1) * 512], in_=e1[:]), inc="st", amt=16)
            p.wait("tensor", "dve", w1)
            p.wait("tensor", "act", w4)
            p.wait("vector", "st")
        woguard[0] = p.cnt["mm"]
        woguard[1] = p.cnt["mm"]
        p.wait("sync", "dve")
    p.wait("gpsimd", "st")
    p.wait("gpsimd", "mm")
    p.wait("gpsimd", "dve")
    p.wait("gpsimd", "act")
    p.finalize()
    return nc, stack


def build_L1a():
    nc = bass.Bass("TRN2", target_bir_lowering=False)
    stack = ExitStack()
    s = stack.enter_context
    x1h = nc.dram_tensor("x1h", [D, HW], F32, kind="ExternalInput").ap()
    wqT = nc.dram_tensor("wqT", [D, D], F32, kind="ExternalInput").ap()
    wkvT = nc.dram_tensor("wkvT", [D, 256], F32, kind="ExternalInput").ap()
    cosK = nc.dram_tensor("cosK", [128, HW], BF16, kind="ExternalInput").ap()
    sinKs = nc.dram_tensor("sinKs", [128, HW], BF16, kind="ExternalInput").ap()
    sigb = nc.dram_tensor("sigb", [128, 64], F32, kind="ExternalInput").ap()
    alph = nc.dram_tensor("alph", [128, T], BF16, kind="ExternalInput").ap()
    xn2T_o = nc.dram_tensor("xn2T_o", [D, T], BF16, kind="ExternalOutput").ap()
    acT_o = nc.dram_tensor("acT_o", [D, T], BF16, kind="ExternalOutput").ap()

    x1c = s(nc.sbuf_tensor("x1c", [128, 1024], F32))
    hi = s(nc.sbuf_tensor("hi", [128, 512], BF16))
    lo = s(nc.sbuf_tensor("lo", [128, 512], BF16))
    sq = s(nc.sbuf_tensor("sq", [128, 512], BF16))
    srow = s(nc.sbuf_tensor("srow", [1, 1536], F32))
    mb = s(nc.sbuf_tensor("mb", [128, HW], BF16))
    rb = s(nc.sbuf_tensor("rb", [128, HW], BF16))
    xn2c = s(nc.sbuf_tensor("xn2c", [128, 1280], BF16))
    xn2T = s(nc.sbuf_tensor("xn2T", [128, DT * T], BF16))
    xn2h = s(nc.sbuf_tensor("xn2h", [128, 512], BF16))
    wkvb = s(nc.sbuf_tensor("wkvb", [128, DT * 128], BF16))
    wkvz = s(nc.sbuf_tensor("wkvz", [128, 1024], F32))
    cosS = s(nc.sbuf_tensor("cosS", [128, HW], BF16))
    sinS = s(nc.sbuf_tensor("sinS", [128, HW], BF16))
    t1 = s(nc.sbuf_tensor("t1", [128, 1024], BF16))
    t2 = s(nc.sbuf_tensor("t2", [128, 1024], BF16))
    kr = s(nc.sbuf_tensor("kr", [128, HW], BF16))
    vstage = s(nc.sbuf_tensor("vstage", [128, 128], BF16))
    vtk = s(nc.sbuf_tensor("vtk", [128, 20 * 128], BF16))
    wqz = [s(nc.sbuf_tensor(f"wqz{i}", [128, 128], F32)) for i in range(2)]
    wqb = [s(nc.sbuf_tensor(f"wqb{i}", [128, 128], BF16)) for i in range(2)]
    qr = s(nc.sbuf_tensor("qr", [128, T], BF16))
    sigb_sb = s(nc.sbuf_tensor("sigb_sb", [128, 64], F32))
    alph_sb = s(nc.sbuf_tensor("alph_sb", [128, T], BF16))
    pTt = s(nc.sbuf_tensor("pTt", [128, 256], BF16))
    abuf = s(nc.sbuf_tensor("abuf", [128, 256], BF16))
    l0 = srow[0:1, 0:256]
    ocn = s(nc.sbuf_tensor("ocn", [128, 256], F32))
    wa = s(nc.sbuf_tensor("wa", [128, 256], F32))
    wbv = s(nc.sbuf_tensor("wbv", [128, 256], F32))
    bl = s(nc.sbuf_tensor("bl", [128, 256], F32))
    blt = s(nc.sbuf_tensor("blt", [128, 256], F32))
    ident = s(nc.sbuf_tensor("ident", [128, 128], BF16))
    ones_b = s(nc.sbuf_tensor("ones_b", [128, 1], BF16))
    ones_f = s(nc.sbuf_tensor("ones_f", [1, 128], F32))
    iota_f = x1c[:, 0:128]
    iota_p = x1c[:, 128:129]

    pQ = s(nc.psum_tensor("pQ", [128, 2048], F32))
    pS_ = s(nc.psum_tensor("pS_", [128, 512], F32))
    pO = s(nc.psum_tensor("pO", [128, 512], F32))
    pB = s(nc.psum_tensor("pB", [128, 256], F32))
    pTr = s(nc.psum_tensor("pTr", [128, 128], BF16))

    p = Prog(nc, stack)
    p.step("gpsimd", lambda e: e.iota(iota_f[:], [[1, 128]], channel_multiplier=0, allow_small_or_imprecise_dtypes=True), inc="gp")
    p.step("gpsimd", lambda e: e.iota(iota_p[:], [[0, 1]], channel_multiplier=1, allow_small_or_imprecise_dtypes=True), inc="gp")
    p.step("gpsimd", lambda e: e.memset(ones_b[:], 1.0), inc="gp")
    p.step("gpsimd", lambda e: e.memset(ones_f[:], 1.0), inc="gp")
    p.step("sync", lambda e: e.dma_start(out=cosS[:], in_=cosK), inc="ld", amt=16)
    p.step("sync", lambda e: e.dma_start(out=sinS[:], in_=sinKs), inc="ld", amt=16)
    p.step("sync", lambda e: e.dma_start(out=sigb_sb[:], in_=sigb), inc="ld", amt=16)
    p.step("sync", lambda e: e.dma_start(out=alph_sb[:], in_=alph), inc="ld", amt=16)
    p.wait("vector", "gp")
    p.step("vector", lambda e: e.tensor_scalar(ident[:], iota_f[:], iota_p[:], None, OP.is_equal), inc="dve")

    for th in range(5):
        for dc in range(DT):
            xld = p.step("sync", lambda e, dc=dc, th=th: e.dma_start(
                out=x1c[:, 0:512], in_=x1h[dc * 128:(dc + 1) * 128, th * 512:(th + 1) * 512]), inc="ld", amt=16)
            p.wait("vector", "ld", xld)
            p.chain("vector", lambda e: e.tensor_copy(hi[:], x1c[:, 0:512]), inc="dve")
            vl = p.chain("vector", lambda e: e.tensor_tensor(lo[:], x1c[:, 0:512], hi[:], OP.subtract), inc="dve")
            p.wait("scalar", "ld", xld)
            sa = p.step("scalar", lambda e: e.activation(sq[:], x1c[:, 0:512], AF.Square), inc="act")
            p.wait("tensor", "dve", vl)
            p.wait("tensor", "act", sa)
            p.step("tensor", lambda e, dc=dc: e.matmul(pS_[0:1, :], ones_b[:], hi[:], start=(dc == 0), stop=False), inc="mm")
            p.step("tensor", lambda e, dc=dc: e.matmul(pS_[0:1, :], ones_b[:], lo[:], start=False, stop=(dc == DT - 1)), inc="mm")
            p.step("tensor", lambda e, dc=dc: e.matmul(pO[0:1, :], ones_b[:], sq[:], start=(dc == 0), stop=(dc == DT - 1)), inc="mm")
            mm_now = p.cnt["mm"]
            p.wait("vector", "mm", mm_now)
            p.wait("scalar", "mm", mm_now)
            p.wait("sync", "dve", vl)
            p.wait("sync", "act", sa)
        mdone = p.cnt["mm"]
        p.wait("vector", "mm", mdone)
        p.chain("vector", lambda e: e.tensor_copy(srow[:, 0:512], pS_[0:1, :]), inc="dve")
        p.chain("vector", lambda e: e.tensor_copy(srow[:, 512:1024], pO[0:1, :]), inc="dve")
        p.chain("vector", lambda e: e.tensor_scalar(srow[:, 0:512], srow[:, 0:512], 1.0 / D, None, OP.mult), inc="dve")
        p.chain("vector", lambda e: e.tensor_tensor(srow[:, 1024:1536], srow[:, 0:512], srow[:, 0:512], OP.mult), inc="dve")
        p.chain("vector", lambda e: e.tensor_scalar(srow[:, 512:1024], srow[:, 512:1024], 1.0 / D, None, OP.mult), inc="dve")
        v6 = p.chain("vector", lambda e: e.tensor_tensor(srow[:, 1024:1536], srow[:, 512:1024], srow[:, 1024:1536], OP.subtract), inc="dve")
        p.wait("scalar", "dve", v6)
        a1 = p.step("scalar", lambda e: e.activation(srow[:, 1024:1536], srow[:, 1024:1536], AF.Sqrt, bias=np.float32(EPS)), inc="act")
        p.wait("vector", "act", a1)
        v7 = p.chain("vector", lambda e: e.reciprocal(srow[:, 1024:1536], srow[:, 1024:1536]), inc="dve")
        p.wait("tensor", "dve", v7)
        for half in range(2):
            m1 = p.step("tensor", lambda e, half=half: e.matmul(pS_[:, 0:256], ones_f[:], srow[0:1, half * 256:half * 256 + 256], start=True, stop=True), inc="mm")
            m2 = p.step("tensor", lambda e, half=half: e.matmul(pO[:, 0:256], ones_f[:], srow[0:1, 1024 + half * 256:1024 + half * 256 + 256], start=True, stop=True), inc="mm")
            p.wait("scalar", "mm", m2)
            a2 = p.step("scalar", lambda e, th=th, half=half: e.activation(mb[:, th * 512 + half * 256: th * 512 + half * 256 + 256], pS_[:, 0:256], AF.Copy), inc="act")
            a3 = p.step("scalar", lambda e, th=th, half=half: e.activation(rb[:, th * 512 + half * 256: th * 512 + half * 256 + 256], pO[:, 0:256], AF.Copy), inc="act")
            p.wait("tensor", "act", a3)

    for dc in range(DT):
        for th in range(2):
            off = 256 + th * 1024
            xld = p.step("sync", lambda e, dc=dc, off=off: e.dma_start(
                out=x1c[:, 0:1024], in_=x1h[dc * 128:(dc + 1) * 128, off:off + 1024]), inc="ld", amt=16)
            p.wait("vector", "ld", xld)
            p.chain("vector", lambda e, off=off: e.tensor_tensor(xn2c[:, 0:1024], x1c[:, 0:1024], mb[:, off:off + 1024], OP.subtract), inc="dve")
            n2 = p.chain("vector", lambda e, dc=dc, th=th, off=off: e.tensor_tensor(
                xn2T[:, dc * T + th * 1024: dc * T + (th + 1) * 1024], xn2c[:, 0:1024], rb[:, off:off + 1024], OP.mult), inc="dve")
            p.wait("sync", "dve", n2)
        p.wait("gpsimd", "dve", p.cnt["dve"])
        p.step("gpsimd", lambda e, dc=dc: e.dma_start(out=xn2T_o[dc * 128:(dc + 1) * 128, :], in_=xn2T[:, dc * T:(dc + 1) * T]), inc="st", amt=16)

    kvguard = 0
    for kv in range(2):
        for g in range(4):
            lw = p.step("sync", lambda e, g=g, kv=kv: e.dma_start(
                out=wkvz[:].rearrange("p (c o) -> p c o", o=128),
                in_=wkvT.rearrange("(c p) o -> p c o", p=128)[:, g * 8:(g + 1) * 8, kv * 128:(kv + 1) * 128]), inc="ld", amt=16)
            p.wait("scalar", "ld", lw)
            if kv == 1:
                p.wait("scalar", "mm", kvguard)
            cw = p.step("scalar", lambda e, g=g: e.copy(wkvb[:, g * 1024:(g + 1) * 1024], wkvz[:]), inc="act")
            p.wait("sync", "act", cw)
        p.wait("tensor", "act")
        for dc in range(DT):
            hld = p.step("sync", lambda e, dc=dc: e.dma_start(out=x1c[:, 0:256], in_=x1h[dc * 128:(dc + 1) * 128, 0:256]), inc="ld", amt=16)
            hld = p.step("sync", lambda e, dc=dc: e.dma_start(out=x1c[:, 256:512], in_=x1h[dc * 128:(dc + 1) * 128, 2304:2560]), inc="ld", amt=16)
            p.wait("vector", "ld", hld)
            p.chain("vector", lambda e: e.tensor_tensor(xn2c[:, 0:256], x1c[:, 0:256], mb[:, 0:256], OP.subtract), inc="dve")
            p.chain("vector", lambda e: e.tensor_tensor(xn2h[:, 0:256], xn2c[:, 0:256], rb[:, 0:256], OP.mult), inc="dve")
            p.chain("vector", lambda e: e.tensor_tensor(xn2c[:, 0:256], x1c[:, 256:512], mb[:, 2304:2560], OP.subtract), inc="dve")
            nh = p.chain("vector", lambda e: e.tensor_tensor(xn2h[:, 256:512], xn2c[:, 0:256], rb[:, 2304:2560], OP.mult), inc="dve")
            p.wait("tensor", "dve", nh)
            p.wait("sync", "dve", nh)
            for cc in range(4):
                p.step("tensor", lambda e, dc=dc, cc=cc: e.matmul(
                    pQ[:, cc * 512:(cc + 1) * 512], wkvb[:, dc * 128:(dc + 1) * 128],
                    xn2T[:, dc * T + cc * 512: dc * T + (cc + 1) * 512],
                    start=(dc == 0), stop=(dc == DT - 1)), inc="mm")
            p.step("tensor", lambda e, dc=dc: e.matmul(
                pS_[:], wkvb[:, dc * 128:(dc + 1) * 128], xn2h[:],
                start=(dc == 0), stop=(dc == DT - 1)), inc="mm")
        mdone = p.cnt["mm"]
        kvguard = mdone
        p.wait("vector", "mm", mdone)
        p.wait("scalar", "mm", mdone)
        if kv == 0:
            for hf in range(2):
                o1, o2 = hf * 1024, 256 + hf * 1024
                p.chain("vector", lambda e, o1=o1, o2=o2: e.tensor_tensor(t1[:], pQ[:, o1:o1 + 1024], cosS[:, o2:o2 + 1024], OP.mult), inc="dve")
                p.chain("vector", lambda e, o1=o1, o2=o2: e.tensor_tensor(t2[0:64, :], pQ[64:128, o1:o1 + 1024], sinS[0:64, o2:o2 + 1024], OP.mult), inc="dve")
                p.chain("vector", lambda e, o1=o1, o2=o2: e.tensor_tensor(t2[64:128, :], pQ[0:64, o1:o1 + 1024], sinS[64:128, o2:o2 + 1024], OP.mult), inc="dve")
                p.chain("vector", lambda e, o2=o2: e.tensor_tensor(kr[:, o2:o2 + 1024], t1[:], t2[:], OP.add), inc="dve")
            p.chain("vector", lambda e: e.tensor_tensor(t1[:, 0:256], pS_[:, 0:256], cosS[:, 0:256], OP.mult), inc="dve")
            p.chain("vector", lambda e: e.tensor_tensor(t2[0:64, 0:256], pS_[64:128, 0:256], sinS[0:64, 0:256], OP.mult), inc="dve")
            p.chain("vector", lambda e: e.tensor_tensor(t2[64:128, 0:256], pS_[0:64, 0:256], sinS[64:128, 0:256], OP.mult), inc="dve")
            p.chain("vector", lambda e: e.tensor_tensor(kr[:, 0:256], t1[:, 0:256], t2[:, 0:256], OP.add), inc="dve")
            p.chain("vector", lambda e: e.tensor_tensor(t1[:, 0:256], pS_[:, 256:512], cosS[:, 2304:2560], OP.mult), inc="dve")
            p.chain("vector", lambda e: e.tensor_tensor(t2[0:64, 0:256], pS_[64:128, 256:512], sinS[0:64, 2304:2560], OP.mult), inc="dve")
            p.chain("vector", lambda e: e.tensor_tensor(t2[64:128, 0:256], pS_[0:64, 256:512], sinS[64:128, 2304:2560], OP.mult), inc="dve")
            vk = p.chain("vector", lambda e: e.tensor_tensor(kr[:, 2304:2560], t1[:, 0:256], t2[:, 0:256], OP.add), inc="dve")
            p.wait("tensor", "dve", vk)
        else:
            for ch in range(20):
                if ch < 2:
                    src = pS_[:, ch * 128:(ch + 1) * 128]
                elif ch < 18:
                    src = pQ[:, (ch - 2) * 128:(ch - 1) * 128]
                else:
                    src = pS_[:, 256 + (ch - 18) * 128:256 + (ch - 17) * 128]
                av = p.step("scalar", lambda e, src=src: e.activation(vstage[:], src, AF.Copy), inc="act")
                p.wait("tensor", "act", av)
                mt = p.step("tensor", lambda e: e.transpose(pTr[:], vstage[:], ident[:]), inc="mm")
                p.wait("scalar", "mm", mt)
                at = p.step("scalar", lambda e, ch=ch: e.activation(vtk[:, ch * 128:(ch + 1) * 128], pTr[:], AF.Copy), inc="act")
                p.wait("tensor", "act", at)

    qg_guard = {}
    for h in range(32):
        for dc in range(DT):
            par = dc % 2
            wld = p.step("sync", lambda e, h=h, dc=dc, par=par: e.dma_start(
                out=wqz[par][:], in_=wqT[dc * 128:(dc + 1) * 128, h * 128:(h + 1) * 128]), inc="ld", amt=16)
            p.wait("scalar", "ld", wld)
            if par in qg_guard:
                p.wait("scalar", "mm", qg_guard[par])
            cv = p.step("scalar", lambda e, par=par: e.copy(wqb[par][:], wqz[par][:]), inc="act")
            p.wait("tensor", "act", cv)
            for cc in range(4):
                p.step("tensor", lambda e, dc=dc, cc=cc, par=par: e.matmul(
                    pQ[:, cc * 512:(cc + 1) * 512], wqb[par][:], xn2T[:, dc * T + cc * 512: dc * T + (cc + 1) * 512],
                    start=(dc == 0), stop=(dc == DT - 1)), inc="mm")
            qg_guard[par] = p.cnt["mm"]
            p.wait("sync", "act", cv)
        qdone = p.cnt["mm"]
        p.wait("vector", "mm", qdone)
        for hf in range(2):
            o1, o2 = hf * 1024, 256 + hf * 1024
            p.chain("vector", lambda e, o1=o1, o2=o2: e.tensor_tensor(t1[:], pQ[:, o1:o1 + 1024], cosS[:, o2:o2 + 1024], OP.mult), inc="dve")
            p.chain("vector", lambda e, o1=o1, o2=o2: e.tensor_tensor(t2[0:64, :], pQ[64:128, o1:o1 + 1024], sinS[0:64, o2:o2 + 1024], OP.mult), inc="dve")
            p.chain("vector", lambda e, o1=o1, o2=o2: e.tensor_tensor(t2[64:128, :], pQ[0:64, o1:o1 + 1024], sinS[64:128, o2:o2 + 1024], OP.mult), inc="dve")
            r4 = p.chain("vector", lambda e, o1=o1: e.tensor_tensor(qr[:, o1:o1 + 1024], t1[:], t2[:], OP.add), inc="dve")
        p.wait("tensor", "dve", r4)
        for b in range(8):
            for w in range(2):
                k0 = (b + w) * 256
                for kc in range(4):
                    mS = p.step("tensor", lambda e, k0=k0, kc=kc, b=b: e.matmul(
                        pS_[:, 0:256], kr[:, k0 + kc * 128:k0 + (kc + 1) * 128], qr[:, b * 256:(b + 1) * 256],
                        start=True, stop=True), inc="mm")
                    idx = b * 8 + w * 4 + kc
                    p.wait("scalar", "mm", mS)
                    aS = p.step("scalar", lambda e, idx=idx: e.activation(pTt[:], pS_[:, 0:256], AF.Sigmoid, bias=sigb_sb[:, idx:idx + 1], scale=np.float32(1.0 / np.sqrt(128.0))), inc="act")
                    p.wait("tensor", "act", aS)
                    p.step("tensor", lambda e, kc=kc: e.matmul(pB[0:1, :], ones_b[:], pTt[:], start=(kc == 0), stop=(kc == 3)), inc="mm")
                    mA = p.step("tensor", lambda e, k0=k0, kc=kc: e.matmul(
                        pO[:, 0:256], vtk[:, (k0 // 128 + kc) * 128:(k0 // 128 + kc + 1) * 128], pTt[:],
                        start=(kc == 0), stop=(kc == 3)), inc="mm")
                    p.wait("scalar", "mm", mA)
                wdone = p.cnt["mm"]
                p.wait("vector", "mm", wdone)
                v1 = p.chain("vector", lambda e: e.reciprocal(l0[:], pB[0:1, :]), inc="dve")
                p.wait("tensor", "dve", v1)
                mB = p.step("tensor", lambda e: e.matmul(pB[:], ones_f[:], l0[:], start=True, stop=True), inc="mm")
                p.wait("scalar", "mm", wdone)
                aO = p.step("scalar", lambda e: e.activation(ocn[:], pO[:, 0:256], AF.Copy), inc="act")
                p.wait("vector", "act", aO)
                p.wait("vector", "mm", mB)
                dst = wa if w == 0 else wbv
                vN = p.step("vector", lambda e, dst=dst: e.tensor_tensor(dst[:], ocn[:], pB[:], OP.mult), inc="dve")
                p.wait("tensor", "dve", vN)
                p.wait("scalar", "dve", vN)
            p.chain("vector", lambda e, b=b: e.tensor_scalar(blt[:], alph_sb[:, b * 256:(b + 1) * 256], -1.0, 1.0, OP.mult, OP.add), inc="dve")
            p.chain("vector", lambda e: e.tensor_tensor(bl[:], wa[:], blt[:], OP.mult), inc="dve")
            p.chain("vector", lambda e, b=b: e.tensor_tensor(blt[:], wbv[:], alph_sb[:, b * 256:(b + 1) * 256], OP.mult), inc="dve")
            p.chain("vector", lambda e: e.tensor_tensor(bl[:], bl[:], blt[:], OP.add), inc="dve")
            b4 = p.chain("vector", lambda e: e.tensor_copy(abuf[:], bl[:]), inc="dve")
            p.wait("gpsimd", "dve", b4)
            gst = p.step("gpsimd", lambda e, h=h, b=b: e.dma_start(out=acT_o[h * 128:(h + 1) * 128, b * 256:(b + 1) * 256], in_=abuf[:]), inc="st", amt=16)
            p.wait("vector", "st", gst)
    p.wait("gpsimd", "st")
    p.wait("gpsimd", "mm")
    p.wait("gpsimd", "dve")
    p.wait("gpsimd", "act")
    p.finalize()
    return nc, stack


def build_L1b():
    nc = bass.Bass("TRN2", target_bir_lowering=False)
    stack = ExitStack()
    s = stack.enter_context
    acT = nc.dram_tensor("acT", [D, T], BF16, kind="ExternalInput").ap()
    x1T = nc.dram_tensor("x1T", [D, T], F32, kind="ExternalInput").ap()
    woT = nc.dram_tensor("woT", [D, D], F32, kind="ExternalInput").ap()
    pt_o = nc.dram_tensor("pt_o", [D, T], F32, kind="ExternalOutput").ap()

    ac = s(nc.sbuf_tensor("ac", [128, DT * T], BF16))
    wz = [s(nc.sbuf_tensor(f"wz{i}", [128, 128], F32)) for i in range(2)]
    wb = [s(nc.sbuf_tensor(f"wb{i}", [128, 128], BF16)) for i in range(2)]
    xts = s(nc.sbuf_tensor("xts", [128, T], F32))
    ev = s(nc.sbuf_tensor("ev", [128, 512], F32))
    pW = s(nc.psum_tensor("pW", [128, 2048], F32))

    p = Prog(nc, stack)
    for dc in range(DT):
        p.step("sync", lambda e, dc=dc: e.dma_start(out=ac[:, dc * T:(dc + 1) * T], in_=acT[dc * 128:(dc + 1) * 128, :]), inc="ld", amt=16)
    acld = p.cnt["ld"]
    p.wait("tensor", "ld", acld)
    guard = {}
    for ot in range(DT):
        xld = p.step("sync", lambda e, ot=ot: e.dma_start(out=xts[:], in_=x1T[ot * 128:(ot + 1) * 128, :]), inc="ld", amt=16)
        for dc in range(DT):
            par = dc % 2
            wld = p.step("sync", lambda e, ot=ot, dc=dc, par=par: e.dma_start(out=wz[par][:], in_=woT[dc * 128:(dc + 1) * 128, ot * 128:(ot + 1) * 128]), inc="ld", amt=16)
            p.wait("scalar", "ld", wld)
            if par in guard:
                p.wait("scalar", "mm", guard[par])
            cv = p.step("scalar", lambda e, par=par: e.copy(wb[par][:], wz[par][:]), inc="act")
            p.wait("tensor", "act", cv)
            for cc in range(4):
                p.step("tensor", lambda e, dc=dc, cc=cc, par=par: e.matmul(
                    pW[:, cc * 512:(cc + 1) * 512], wb[par][:], ac[:, dc * T + cc * 512: dc * T + (cc + 1) * 512],
                    start=(dc == 0), stop=(dc == DT - 1)), inc="mm")
            guard[par] = p.cnt["mm"]
            p.wait("sync", "act", cv)
        mdone = p.cnt["mm"]
        p.wait("vector", "mm", mdone)
        p.wait("vector", "ld", xld)
        for cc in range(4):
            vv = p.chain("vector", lambda e, cc=cc: e.tensor_tensor(ev[:], pW[:, cc * 512:(cc + 1) * 512], xts[:, cc * 512:(cc + 1) * 512], OP.add), inc="dve")
            p.wait("gpsimd", "dve", vv)
            gst = p.step("gpsimd", lambda e, ot=ot, cc=cc: e.dma_start(out=pt_o[ot * 128:(ot + 1) * 128, cc * 512:(cc + 1) * 512], in_=ev[:]), inc="st", amt=16)
            p.wait("vector", "st", gst)
        p.wait("tensor", "dve")
        p.wait("sync", "dve")
    p.wait("gpsimd", "st")
    p.wait("gpsimd", "mm")
    p.wait("gpsimd", "dve")
    p.wait("gpsimd", "act")
    p.finalize()
    return nc, stack


def build_L1c():
    nc = bass.Bass("TRN2", target_bir_lowering=False)
    stack = ExitStack()
    s = stack.enter_context
    xn2T = nc.dram_tensor("xn2T", [D, T], BF16, kind="ExternalInput").ap()
    part = nc.dram_tensor("part", [D, T], F32, kind="ExternalInput").ap()
    wbT = nc.dram_tensor("wbT", [D, D], F32, kind="ExternalInput").ap()
    bco = nc.dram_tensor("bco", [128, DT], F32, kind="ExternalInput").ap()
    wgT = nc.dram_tensor("wgT", [D, 8], F32, kind="ExternalInput").ap()
    wtr = nc.dram_tensor("wtr", [128, DT], F32, kind="ExternalInput").ap()
    btr = nc.dram_tensor("btr", [1, 1], F32, kind="ExternalInput").ap()
    aaT = nc.dram_tensor("aaT", [D, 128], F32, kind="ExternalInput").ap()
    baT = nc.dram_tensor("baT", [128, D], F32, kind="ExternalInput").ap()
    eex = nc.dram_tensor("eex", [8, 128], BF16, kind="ExternalInput").ap()
    out_o = nc.dram_tensor("out_o", [T, D], F32, kind="ExternalOutput").ap()

    xn2 = s(nc.sbuf_tensor("xn2", [128, DT * T], BF16))
    wgz = s(nc.sbuf_tensor("wgz", [128, DT * 8], F32))
    wgb = s(nc.sbuf_tensor("wgb", [128, DT * 8], BF16))
    wtz = s(nc.sbuf_tensor("wtz", [128, DT], F32))
    wtb = s(nc.sbuf_tensor("wtb", [128, DT], BF16))
    aaz = s(nc.sbuf_tensor("aaz", [128, 128], F32))
    aab = s(nc.sbuf_tensor("aab", [128, DT * 128], BF16))
    bab = s(nc.sbuf_tensor("bab", [128, DT * 128], BF16))
    eeb = s(nc.sbuf_tensor("eeb", [8, 128], BF16))
    btr_sb = s(nc.sbuf_tensor("btr_sb", [1, 1], F32))
    bco_sb = s(nc.sbuf_tensor("bco_sb", [128, DT], F32))
    g8 = s(nc.sbuf_tensor("g8", [8, T], BF16))
    thr_r = s(nc.sbuf_tensor("thr_r", [1, T], BF16))
    gate_t = s(nc.sbuf_tensor("gate_t", [128, 8], F32))
    ge = s(nc.sbuf_tensor("ge", [128, 8], F32))
    gmx = s(nc.sbuf_tensor("gmx", [128, 6], F32))
    thr_t = s(nc.sbuf_tensor("thr_t", [128, 1], F32))
    wtok = s(nc.sbuf_tensor("wtok", [128, 8], F32))
    wtkb = s(nc.sbuf_tensor("wtkb", [128, 8], BF16))
    wT8 = s(nc.sbuf_tensor("wT8", [8, T], BF16))
    we = s(nc.sbuf_tensor("we", [128, T], BF16))
    hT = s(nc.sbuf_tensor("hT", [128, T], BF16))
    gsb = s(nc.sbuf_tensor("gsb", [128, T], BF16))
    wz = [s(nc.sbuf_tensor(f"wz{i}", [128, 128], F32)) for i in range(2)]
    wb = [s(nc.sbuf_tensor(f"wb{i}", [128, 128], BF16)) for i in range(2)]
    pts = s(nc.sbuf_tensor("pts", [128, T], F32))
    x2c = s(nc.sbuf_tensor("x2c", [128, T], F32))
    m1s = s(nc.sbuf_tensor("m1s", [128, 512], F32))
    hi = s(nc.sbuf_tensor("hi", [128, T], BF16))
    lo = s(nc.sbuf_tensor("lo", [128, T], BF16))
    tt1 = s(nc.sbuf_tensor("tt1", [128, 128], F32))
    tt2 = s(nc.sbuf_tensor("tt2", [128, 128], F32))
    ident = s(nc.sbuf_tensor("ident", [128, 128], BF16))
    iota_f = s(nc.sbuf_tensor("iota_f", [128, 128], F32))
    iota_p = s(nc.sbuf_tensor("iota_p", [128, 1], F32))

    pG = s(nc.psum_tensor("pG", [128, 512], F32))
    pH = s(nc.psum_tensor("pH", [128, 512], F32))
    pM = s(nc.psum_tensor("pM", [128, 1024], F32))
    pB_ = s(nc.psum_tensor("pB_", [128, 1024], F32))
    pt1 = s(nc.psum_tensor("pt1", [128, 128], BF16))
    pt2 = s(nc.psum_tensor("pt2", [128, 128], BF16))

    p = Prog(nc, stack)
    p.step("gpsimd", lambda e: e.iota(iota_f[:], [[1, 128]], channel_multiplier=0, allow_small_or_imprecise_dtypes=True), inc="gp")
    p.step("gpsimd", lambda e: e.iota(iota_p[:], [[0, 1]], channel_multiplier=1, allow_small_or_imprecise_dtypes=True), inc="gp")
    p.step("sync", lambda e: e.dma_start(out=btr_sb[:], in_=btr), inc="ld", amt=16)
    p.step("sync", lambda e: e.dma_start(out=bco_sb[:], in_=bco), inc="ld", amt=16)
    p.step("sync", lambda e: e.dma_start(out=eeb[:], in_=eex), inc="ld", amt=16)
    p.step("sync", lambda e: e.dma_start(out=wgz[:].rearrange("p (c o) -> p c o", o=8), in_=wgT.rearrange("(c p) o -> p c o", p=128)), inc="ld", amt=16)
    p.step("sync", lambda e: e.dma_start(out=wtz[:], in_=wtr), inc="ld", amt=16)
    for dc in range(DT):
        p.step("sync", lambda e, dc=dc: e.dma_start(out=xn2[:, dc * T:(dc + 1) * T], in_=xn2T[dc * 128:(dc + 1) * 128, :]), inc="ld", amt=16)
    ld0 = p.cnt["ld"]
    p.wait("vector", "gp")
    p.step("vector", lambda e: e.tensor_scalar(ident[:], iota_f[:], iota_p[:], None, OP.is_equal), inc="dve")
    p.wait("scalar", "ld", ld0)
    p.step("scalar", lambda e: e.copy(wgb[:], wgz[:]), inc="act")
    p.step("scalar", lambda e: e.copy(wtb[:], wtz[:]), inc="act")
    for dc in range(DT):
        l1_ = p.step("sync", lambda e, dc=dc: e.dma_start(out=aaz[:], in_=aaT[dc * 128:(dc + 1) * 128, :]), inc="ld", amt=16)
        p.wait("scalar", "ld", l1_)
        ca = p.step("scalar", lambda e, dc=dc: e.copy(aab[:, dc * 128:(dc + 1) * 128], aaz[:]), inc="act")
        p.wait("sync", "act", ca)
        l2_ = p.step("sync", lambda e, dc=dc: e.dma_start(out=aaz[:], in_=baT[:, dc * 128:(dc + 1) * 128]), inc="ld", amt=16)
        p.wait("scalar", "ld", l2_)
        cb = p.step("scalar", lambda e, dc=dc: e.copy(bab[:, dc * 128:(dc + 1) * 128], aaz[:]), inc="act")
        p.wait("sync", "act", cb)
    cdone = p.cnt["act"]
    p.wait("tensor", "act", cdone)
    p.wait("tensor", "dve")

    for tc in range(4):
        for dc in range(DT):
            p.step("tensor", lambda e, dc=dc, tc=tc: e.matmul(
                pG[0:8, :], wgb[:, dc * 8:(dc + 1) * 8], xn2[:, dc * T + tc * 512: dc * T + (tc + 1) * 512],
                start=(dc == 0), stop=(dc == DT - 1)), inc="mm")
        g_done = p.cnt["mm"]
        for dc in range(DT):
            p.step("tensor", lambda e, dc=dc, tc=tc: e.matmul(
                pH[0:1, :], wtb[:, dc:dc + 1], xn2[:, dc * T + tc * 512: dc * T + (tc + 1) * 512],
                start=(dc == 0), stop=(dc == DT - 1)), inc="mm")
        t_done = p.cnt["mm"]
        for dc in range(DT):
            p.step("tensor", lambda e, dc=dc, tc=tc: e.matmul(
                pM[:, 0:512], aab[:, dc * 128:(dc + 1) * 128], xn2[:, dc * T + tc * 512: dc * T + (tc + 1) * 512],
                start=(dc == 0), stop=(dc == DT - 1)), inc="mm")
        h_done = p.cnt["mm"]
        p.wait("scalar", "mm", g_done)
        a1 = p.step("scalar", lambda e, tc=tc: e.activation(g8[0:8, tc * 512:(tc + 1) * 512], pG[0:8, :], AF.Copy), inc="act")
        p.wait("scalar", "mm", t_done)
        a2 = p.step("scalar", lambda e, tc=tc: e.activation(thr_r[:, tc * 512:(tc + 1) * 512], pH[0:1, :], AF.Sigmoid, bias=btr_sb[0:1, 0:1]), inc="act")
        p.wait("scalar", "mm", h_done)
        a3 = p.step("scalar", lambda e, tc=tc: e.activation(hT[:, tc * 512:(tc + 1) * 512], pM[:, 0:512], AF.Copy), inc="act")
        p.wait("tensor", "act", a3)

    for tt in range(16):
        mt = p.step("tensor", lambda e, tt=tt: e.transpose(pt1[:, 0:8], g8[0:8, tt * 128:(tt + 1) * 128], ident[0:8, 0:8]), inc="mm")
        mt2 = p.step("tensor", lambda e, tt=tt: e.transpose(pt2[:, 0:1], thr_r[0:1, tt * 128:(tt + 1) * 128], ident[0:1, 0:1]), inc="mm")
        p.wait("vector", "mm", mt2)
        p.chain("vector", lambda e: e.tensor_copy(gate_t[:], pt1[:, 0:8]), inc="dve")
        p.chain("vector", lambda e: e.tensor_copy(thr_t[:], pt2[:, 0:1]), inc="dve")
        p.chain("vector", lambda e: e.reduce_max(gmx[:, 0:1], gate_t[:], axis=AX.X), inc="dve")
        v3 = p.chain("vector", lambda e: e.tensor_scalar(gmx[:, 1:2], gmx[:, 0:1], -1.0, None, OP.mult), inc="dve")
        p.wait("tensor", "dve", v3)
        p.wait("scalar", "dve", v3)
        a1 = p.step("scalar", lambda e: e.activation(ge[:], gate_t[:], AF.Exp, bias=gmx[:, 1:2]), inc="act")
        p.wait("vector", "act", a1)
        p.chain("vector", lambda e: e.reduce_sum(gmx[:, 2:3], ge[:], axis=AX.X), inc="dve")
        p.chain("vector", lambda e: e.tensor_scalar(gmx[:, 3:4], gmx[:, 2:3], thr_t[:], 0.125, OP.mult, OP.mult), inc="dve")
        p.chain("vector", lambda e: e.tensor_scalar(wtok[:], ge[:], gmx[:, 3:4], 0.0, OP.subtract, OP.max), inc="dve")
        p.chain("vector", lambda e: e.reduce_sum(gmx[:, 4:5], wtok[:], axis=AX.X), inc="dve")
        p.chain("vector", lambda e: e.tensor_scalar(gmx[:, 5:6], gmx[:, 4:5], 0.0, None, OP.is_equal), inc="dve")
        p.chain("vector", lambda e: e.tensor_tensor(gmx[:, 4:5], gmx[:, 4:5], gmx[:, 5:6], OP.add), inc="dve")
        p.chain("vector", lambda e: e.reciprocal(gmx[:, 5:6], gmx[:, 4:5]), inc="dve")
        v11 = p.chain("vector", lambda e: e.tensor_scalar(wtkb[:], wtok[:], gmx[:, 5:6], 2.0, OP.mult, OP.mult), inc="dve")
        p.wait("tensor", "dve", v11)
        mt3 = p.step("tensor", lambda e: e.transpose(pt2[0:8, :], wtkb[:], ident[:]), inc="mm")
        p.wait("scalar", "mm", mt3)
        a2 = p.step("scalar", lambda e, tt=tt: e.activation(wT8[0:8, tt * 128:(tt + 1) * 128], pt2[0:8, :], AF.Copy), inc="act")
        p.wait("tensor", "act", a2)
        p.wait("vector", "act", a2)
    for tc in range(4):
        mwe = p.step("tensor", lambda e, tc=tc: e.matmul(pM[:, 0:512], eeb[:], wT8[0:8, tc * 512:(tc + 1) * 512], start=True, stop=True), inc="mm")
        p.wait("scalar", "mm", mwe)
        awe = p.step("scalar", lambda e, tc=tc: e.activation(we[:, tc * 512:(tc + 1) * 512], pM[:, 0:512], AF.Copy), inc="act")
        p.wait("tensor", "act", awe)
    p.wait("vector", "act")
    vg = p.chain("vector", lambda e: e.tensor_tensor(gsb[:], hT[:], we[:], OP.mult), inc="dve")
    p.wait("tensor", "dve", vg)

    guard = {}
    mguard = 0
    for ot in range(DT):
        pld = p.step("sync", lambda e, ot=ot: e.dma_start(out=pts[:], in_=part[ot * 128:(ot + 1) * 128, :]), inc="ld", amt=16)
        for dc in range(DT):
            par = dc % 2
            wld = p.step("sync", lambda e, ot=ot, dc=dc, par=par: e.dma_start(out=wz[par][:], in_=wbT[dc * 128:(dc + 1) * 128, ot * 128:(ot + 1) * 128]), inc="ld", amt=16)
            p.wait("scalar", "ld", wld)
            if par in guard:
                p.wait("scalar", "mm", guard[par])
            cv = p.step("scalar", lambda e, par=par: e.copy(wb[par][:], wz[par][:]), inc="act")
            p.wait("tensor", "act", cv)
            for cc in range(2):
                p.step("tensor", lambda e, dc=dc, cc=cc, par=par: e.matmul(
                    pB_[:, cc * 512:(cc + 1) * 512], wb[par][:], xn2[:, dc * T + cc * 512: dc * T + (cc + 1) * 512],
                    start=(dc == 0), stop=(dc == DT - 1)), inc="mm")
            for cc in range(2):
                p.step("tensor", lambda e, dc=dc, cc=cc, par=par: e.matmul(
                    pM[:, cc * 512:(cc + 1) * 512], wb[par][:], xn2[:, dc * T + (2 + cc) * 512: dc * T + (3 + cc) * 512],
                    start=(dc == 0), stop=(dc == DT - 1)), inc="mm")
            guard[par] = p.cnt["mm"]
            p.wait("sync", "act", cv)
        p.wait("vector", "ld", pld)
        for cc in range(4):
            pmo = pG if cc % 2 == 0 else pH
            pbase = pB_ if cc < 2 else pM
            off = (cc % 2) * 512
            mmo = p.step("tensor", lambda e, ot=ot, cc=cc, pmo=pmo: e.matmul(
                pmo[:], bab[:, ot * 128:(ot + 1) * 128], gsb[:, cc * 512:(cc + 1) * 512], start=True, stop=True), inc="mm")
            p.wait("vector", "mm", mmo)
            p.chain("vector", lambda e, ot=ot, pbase=pbase, off=off: e.tensor_scalar(m1s[:], pbase[:, off:off + 512], bco_sb[:, ot:ot + 1], None, OP.add), inc="dve")
            p.chain("vector", lambda e, pmo=pmo: e.tensor_tensor(m1s[:], m1s[:], pmo[:], OP.add), inc="dve")
            z3 = p.chain("vector", lambda e, cc=cc: e.tensor_tensor(x2c[:, cc * 512:(cc + 1) * 512], m1s[:], pts[:, cc * 512:(cc + 1) * 512], OP.add), inc="dve")
            p.wait("tensor", "dve", z3)
        xdone = p.cnt["dve"]
        p.wait("tensor", "dve", xdone)
        p.chain("vector", lambda e: e.tensor_copy(hi[:], x2c[:]), inc="dve")
        vl = p.chain("vector", lambda e: e.tensor_tensor(lo[:], x2c[:], hi[:], OP.subtract), inc="dve")
        p.wait("tensor", "dve", vl)
        for tcc in range(16):
            m1 = p.step("tensor", lambda e, tcc=tcc: e.transpose(pt1[:], hi[:, tcc * 128:(tcc + 1) * 128], ident[:]), inc="mm")
            m2 = p.step("tensor", lambda e, tcc=tcc: e.transpose(pt2[:], lo[:, tcc * 128:(tcc + 1) * 128], ident[:]), inc="mm")
            p.wait("scalar", "mm", m1)
            ac1 = p.step("scalar", lambda e: e.activation(tt1[:], pt1[:], AF.Copy), inc="act")
            p.wait("vector", "mm", m2)
            p.wait("vector", "act", ac1)
            vv = p.step("vector", lambda e: e.tensor_tensor(tt2[:], tt1[:], pt2[:], OP.add), inc="dve")
            p.wait("gpsimd", "dve", vv)
            gst = p.step("gpsimd", lambda e, ot=ot, tcc=tcc: e.dma_start(out=out_o[tcc * 128:(tcc + 1) * 128, ot * 128:(ot + 1) * 128], in_=tt2[:]), inc="st", amt=16)
            p.wait("tensor", "dve", vv)
            p.wait("tensor", "act", ac1)
            p.wait("vector", "st", gst)
            p.wait("scalar", "st", gst)
        p.wait("sync", "dve")
        p.wait("vector", "mm")
    p.wait("gpsimd", "st")
    p.wait("gpsimd", "mm")
    p.wait("gpsimd", "dve")
    p.wait("gpsimd", "act")
    p.finalize()
    return nc, stack


def _cols(v):
    v = np.asarray(v, dtype=np.float32).reshape(-1)
    return np.ascontiguousarray(v.reshape(-1, 128).T)


def kernel(**inp):
    x = np.ascontiguousarray(np.asarray(inp["L0_attn_x"], dtype=np.float32).reshape(S, D))

    he = np.arange(512)
    permq = (he // 64) * 128 + 2 * (he % 64)
    perm = np.concatenate([permq, permq + 1])
    wqk = np.asarray(inp["L0_attn_Wqk_w"], dtype=np.float32)[perm]
    wv0 = np.asarray(inp["L0_attn_Wv_w"], dtype=np.float32)
    wqkvT = np.ascontiguousarray(np.concatenate([wqk, wv0], axis=0).T)
    bias12 = np.ascontiguousarray(np.concatenate(
        [_cols(np.asarray(inp["L0_attn_Wqk_b"], dtype=np.float32)[perm]), _cols(inp["L0_attn_Wv_b"])], axis=1))
    woT0 = np.ascontiguousarray(np.asarray(inp["L0_attn_Wo_w"], dtype=np.float32).T)
    qwr = np.ascontiguousarray(np.repeat(np.asarray(inp["L0_ffn_qweight"]), 8, axis=0))
    scz = np.ascontiguousarray(np.concatenate(
        [_cols(inp["L0_ffn_scales"]), _cols(inp["L0_ffn_zeros"]),
         _cols(inp["L0_ffn_bias"]), _cols(inp["L0_attn_Wo_b"])], axis=1))
    shifts = ((np.arange(128, dtype=np.int32) % 8) * 4).reshape(128, 1)

    wqT1 = np.ascontiguousarray(np.asarray(inp["L1_attn_Wq"], dtype=np.float32).T)
    wkvT1 = np.ascontiguousarray(np.concatenate(
        [np.asarray(inp["L1_attn_Wk"], dtype=np.float32), np.asarray(inp["L1_attn_Wv"], dtype=np.float32)], axis=0).T)
    woT1 = np.ascontiguousarray(np.asarray(inp["L1_attn_Wo"], dtype=np.float32).T)
    wbT1 = np.ascontiguousarray(np.asarray(inp["L1_ffn_W_base"], dtype=np.float32).T)
    bco = _cols(inp["L1_ffn_b_base"])
    wgT1 = np.ascontiguousarray(np.asarray(inp["L1_ffn_W_gate"], dtype=np.float32).T)
    wtr1 = _cols(inp["L1_ffn_W_thr"])
    btr1 = np.ascontiguousarray(np.asarray(inp["L1_ffn_b_thr"], dtype=np.float32).reshape(1, 1))
    aaT = np.ascontiguousarray(np.asarray(inp["L1_ffn_A_w"], dtype=np.float32).transpose(2, 0, 1).reshape(D, 128))
    baT = np.ascontiguousarray(np.asarray(inp["L1_ffn_B_w"], dtype=np.float32).transpose(0, 2, 1).reshape(128, D))
    eex = np.zeros((8, 128), dtype=ml_dtypes.bfloat16)
    for ee in range(8):
        eex[ee, ee * 16:(ee + 1) * 16] = 1.0

    DH = 128
    inv_freq = 1.0 / (10000.0 ** (np.arange(0, DH, 2, dtype=np.float32) / DH))
    cosKs, sinKss, alphas, omas, sigbs = [], [], [], [], []
    for c in range(NC):
        s0 = c * T
        pos = np.arange(s0 - 256, s0 + T + 256, dtype=np.float32)
        f = np.outer(pos, inv_freq)
        cos = np.zeros((HW, DH), dtype=np.float32)
        sin = np.zeros((HW, DH), dtype=np.float32)
        cos[:, 0::2] = np.cos(f); cos[:, 1::2] = np.cos(f)
        sin[:, 0::2] = np.sin(f); sin[:, 1::2] = np.sin(f)
        sgn = np.ones((DH, 1), dtype=np.float32); sgn[:64] = -1.0
        cosKs.append(np.ascontiguousarray(cos.T).astype(ml_dtypes.bfloat16))
        sinKss.append(np.ascontiguousarray(sin.T * sgn).astype(ml_dtypes.bfloat16))
        al = np.zeros((128, T), dtype=np.float32)
        base = np.linspace(0.0, 1.0, 256, dtype=np.float32)
        for b in range(8):
            al[:, b * 256:(b + 1) * 256] = base[None, :]
        if c == 0:
            al[:, 0:256] = 1.0
        alphas.append(np.ascontiguousarray(al).astype(ml_dtypes.bfloat16))
        omas.append(np.ascontiguousarray(1.0 - al).astype(ml_dtypes.bfloat16))
        sg = np.zeros((128, 64), dtype=np.float32)
        if c == NC - 1:
            for kc in (2, 3):
                sg[:, 7 * 8 + 1 * 4 + kc] = -1e9
        sigbs.append(sg)

    nc1, st1 = build_L0a()
    in_maps = [dict(x=np.ascontiguousarray(x[c * T:(c + 1) * T]), wqkvT=wqkvT, bias12=bias12) for c in range(NC)]
    r1 = run_bass_kernel_spmd(nc1, in_maps, core_ids=list(range(NC))).results
    st1.close()
    kTf = np.ascontiguousarray(np.concatenate([r["kT_o"] for r in r1], axis=1))
    vfull = np.ascontiguousarray(np.concatenate([r["v_o"] for r in r1], axis=0))

    nc2, st2 = build_L0b()
    in_maps = [dict(qT=r1[c]["qT_o"], kTf=kTf, vf=vfull, xnTd=r1[c]["xnT_o"], xTd=r1[c]["xT_o"],
                    qwr=qwr, scz=scz, woT=woT0, shifts=shifts) for c in range(NC)]
    r2 = run_bass_kernel_spmd(nc2, in_maps, core_ids=list(range(NC))).results
    st2.close()
    x1T = np.concatenate([np.asarray(r["x1T_o"]) for r in r2], axis=1)

    zpad = np.zeros((D, 256), dtype=np.float32)
    nc3, st3 = build_L1a()
    in_maps = []
    for c in range(NC):
        lo_pad = zpad if c == 0 else x1T[:, c * T - 256:c * T]
        hi_pad = zpad if c == NC - 1 else x1T[:, (c + 1) * T:(c + 1) * T + 256]
        x1h = np.ascontiguousarray(np.concatenate([lo_pad, x1T[:, c * T:(c + 1) * T], hi_pad], axis=1))
        in_maps.append(dict(x1h=x1h, wqT=wqT1, wkvT=wkvT1, cosK=cosKs[c], sinKs=sinKss[c],
                            sigb=sigbs[c], alph=alphas[c], oma=omas[c]))
    r3 = run_bass_kernel_spmd(nc3, in_maps, core_ids=list(range(NC))).results
    st3.close()

    if os.environ.get("KDBG"):
        np.save("/tmp/k_x1T.npy", x1T)
        np.save("/tmp/k_qT.npy", np.asarray(r1[1]["qT_o"], dtype=np.float32))
        np.save("/tmp/k_kTf.npy", np.asarray(kTf, dtype=np.float32))
        np.save("/tmp/k_vf.npy", np.asarray(vfull, dtype=np.float32))
        np.save("/tmp/k_xnT1.npy", np.asarray(r1[1]["xnT_o"], dtype=np.float32))
        np.save("/tmp/k_acT1.npy", np.asarray(r3[1]["acT_o"], dtype=np.float32))
        np.save("/tmp/k_xn2T1.npy", np.asarray(r3[1]["xn2T_o"], dtype=np.float32))
    nc4, st4 = build_L1b()
    in_maps = [dict(acT=r3[c]["acT_o"], x1T=np.ascontiguousarray(x1T[:, c * T:(c + 1) * T]), woT=woT1) for c in range(NC)]
    r4 = run_bass_kernel_spmd(nc4, in_maps, core_ids=list(range(NC))).results
    st4.close()

    if os.environ.get("KDBG"):
        np.save("/tmp/k_pt1.npy", np.asarray(r4[1]["pt_o"], dtype=np.float32))
    nc5, st5 = build_L1c()
    in_maps = [dict(xn2T=r3[c]["xn2T_o"], part=r4[c]["pt_o"], wbT=wbT1, bco=bco, wgT=wgT1,
                    wtr=wtr1, btr=btr1, aaT=aaT, baT=baT, eex=eex) for c in range(NC)]
    r5 = run_bass_kernel_spmd(nc5, in_maps, core_ids=list(range(NC))).results
    st5.close()

    out = np.concatenate([np.asarray(r["out_o"], dtype=np.float32) for r in r5], axis=0)
    return out.reshape(1, S, D)
